# version 02 (final)
# baseline (speedup 1.0000x reference)
import numpy as np
from contextlib import ExitStack

import concourse.bass as bass
import concourse.mybir as mybir
from concourse.bass_utils import run_bass_kernel_spmd

F32 = mybir.dt.float32
OP = mybir.AluOpType
AX = mybir.AxisListType
AF = mybir.ActivationFunctionType

DEBUG_TAP = None
SKIP = frozenset()
N_HALVES = 2
N_EXPERTS = 16
N_C16 = 8
TAP_HALF = None

ENGS = ("tensor", "vector", "scalar", "gpsimd", "sync")
BARENG = ("tensor", "vector", "scalar", "gpsimd")

D = 1024
T = 2048
NH = 16
HTOK = 1024


class Prog:

    def __init__(self):
        self.ops = {e: [] for e in ENGS}
        self.ndma = {e: 0 for e in ENGS}
        self.nbar = 0

    def em(self, eng, fn, deps=()):
        tok = ("c", eng, len(self.ops[eng]))
        dd = [d for d in deps if d is not None]
        self.ops[eng].append({"fn": fn, "deps": dd, "bar": None, "dma": None})
        return tok

    def dma(self, eng, fn, deps=()):
        qi = self.ndma[eng]
        self.ndma[eng] += 1
        tok = ("d", eng, qi)
        dd = [d for d in deps if d is not None]
        self.ops[eng].append({"fn": fn, "deps": dd, "bar": None, "dma": qi})
        return tok

    def barrier(self):
        self.nbar += 1
        for e in BARENG:
            self.ops[e].append({"fn": None, "deps": [], "bar": self.nbar,
                                "dma": None})

    def emit(self, nc, es, vdummy=None):
        csem = {e: es.enter_context(nc.semaphore(f"sC_{e}")) for e in ENGS}
        dsem = {e: es.enter_context(nc.semaphore(f"sD_{e}"))
                for e in ENGS if self.ndma[e]}
        bsem = es.enter_context(nc.semaphore("sBar"))
        marked = {e: set() for e in ENGS}
        for e in ENGS:
            for op in self.ops[e]:
                for (kind, de, di) in op["deps"]:
                    if kind == "c" and de != e:
                        marked[de].add(di)
        cum = {}
        for e in ENGS:
            n = len(self.ops[e])
            arr = np.zeros(n + 1, dtype=np.int64)
            for i in marked[e]:
                arr[i + 1] = 1
            cum[e] = np.cumsum(arr)

        def body(eng, engobj):
            hw = {}

            def wait(sem, v):
                k = sem.name if hasattr(sem, "name") else id(sem)
                if hw.get(k, 0) >= v:
                    return
                engobj.wait_ge(sem, v)
                hw[k] = v

            for i, op in enumerate(self.ops[eng]):
                if op["bar"] is not None:
                    engobj.sem_inc(bsem, 1)
                    wait(bsem, len(BARENG) * op["bar"])
                    continue
                nw0 = len(hw)
                waited = False
                for (kind, de, di) in op["deps"]:
                    if kind == "c":
                        if de == eng:
                            continue
                        v0 = hw.get(csem[de].name if hasattr(csem[de], "name")
                                    else id(csem[de]), 0)
                        wait(csem[de], int(cum[de][di + 1]))
                        if hw.get(csem[de].name if hasattr(csem[de], "name")
                                  else id(csem[de]), 0) != v0:
                            waited = True
                    else:
                        v0 = hw.get(dsem[de].name if hasattr(dsem[de], "name")
                                    else id(dsem[de]), 0)
                        wait(dsem[de], 16 * (di + 1))
                        if hw.get(dsem[de].name if hasattr(dsem[de], "name")
                                  else id(dsem[de]), 0) != v0:
                            waited = True
                if waited and eng == "vector" and vdummy is not None:
                    engobj.tensor_copy(vdummy[0:1, 0:2], vdummy[0:1, 2:4])
                inst = op["fn"](engobj)
                if op["dma"] is not None:
                    inst.then_inc(dsem[eng], 16)
                elif i in marked[eng]:
                    if inst is not None:
                        inst.then_inc(csem[eng], 1)
                    else:
                        engobj.sem_inc(csem[eng], 1)
        return body


def build_nc():
    nc = bass.Bass("TRN2", target_bir_lowering=False)
    es = ExitStack()

    xt = nc.dram_tensor("xt", [D, T], F32, kind="ExternalInput").ap()
    wq = nc.dram_tensor("wq", [D, D], F32, kind="ExternalInput").ap()
    wk = nc.dram_tensor("wk", [D, D], F32, kind="ExternalInput").ap()
    wv = nc.dram_tensor("wv", [D, D], F32, kind="ExternalInput").ap()
    wo = nc.dram_tensor("wo", [D, D], F32, kind="ExternalInput").ap()
    wbet = nc.dram_tensor("wbet", [D, NH], F32, kind="ExternalInput").ap()
    wf = nc.dram_tensor("wf", [D, NH], F32, kind="ExternalInput").ap()
    nwc = nc.dram_tensor("nwc", [128, 8], F32, kind="ExternalInput").ap()
    gwt = nc.dram_tensor("gwt", [D, NH], F32, kind="ExternalInput").ap()
    oneb = nc.dram_tensor("oneb", [128, 2], F32, kind="ExternalInput").ap()
    ebrow = nc.dram_tensor("ebrow", [128, NH], F32, kind="ExternalInput").ap()
    wg = nc.dram_tensor("wg", [16, D, D], F32, kind="ExternalInput").ap()
    wu = nc.dram_tensor("wu", [16, D, D], F32, kind="ExternalInput").ap()
    wd = nc.dram_tensor("wd", [16, D, D], F32, kind="ExternalInput").ap()
    sg = nc.dram_tensor("sg", [D, D], F32, kind="ExternalInput").ap()
    su = nc.dram_tensor("su", [D, D], F32, kind="ExternalInput").ap()
    sd = nc.dram_tensor("sd", [D, D], F32, kind="ExternalInput").ap()
    out = nc.dram_tensor("out", [D, T], F32, kind="ExternalOutput").ap()
    dbg = None
    if DEBUG_TAP in ("xn", "xnpre", "xnboth"):
        dbg = nc.dram_tensor("dbg", [D, T], F32, kind="ExternalOutput").ap()
    elif DEBUG_TAP == "rows":
        dbg = nc.dram_tensor("dbg", [32, T], F32, kind="ExternalOutput").ap()
    elif DEBUG_TAP == "qkv":
        dbg = nc.dram_tensor("dbg", [128, 12288], F32, kind="ExternalOutput").ap()
    elif DEBUG_TAP == "gating":
        dbg = nc.dram_tensor("dbg", [128, 4096], F32, kind="ExternalOutput").ap()

    ident_np = np.eye(128, dtype=np.float32)
    maski_np = np.where(np.arange(128)[:, None] <= np.arange(128)[None, :],
                        np.float32(0.0), np.float32(-1e30)).astype(np.float32)
    selk_np = np.zeros((128, 128), dtype=np.float32)
    for c in range(8):
        for dloc in range(128):
            selk_np[dloc, c * 16 + 2 * c + dloc // 64] = 1.0
    rep_np = np.zeros((16, 1024), dtype=np.float32)
    for c in range(8):
        for m in range(128):
            rep_np[2 * c + m // 64, c * 128 + m] = 1.0
    identT = nc.inline_tensor(ident_np, "identc")
    maskiT = nc.inline_tensor(maski_np, "maskic")
    selkT = nc.inline_tensor(selk_np, "selkc")
    repT = nc.inline_tensor(rep_np, "repc")
    ones1T = nc.inline_tensor(np.ones((1, 128), dtype=np.float32), "ones1c")
    constc_np = np.broadcast_to(
        np.array([1e-5, 1e-12, 1e-6, 1.0 / 64.0], dtype=np.float32),
        (128, 4)).copy()
    constcT = nc.inline_tensor(constc_np, "constc")

    sb = lambda name, shape: es.enter_context(nc.sbuf_tensor(name, shape, F32))
    XN = sb("XN", [128, 8192])
    ACC = sb("ACC", [128, 8192])
    W = [sb(f"W{i}", [128, 4096]) for i in range(6)]
    AB = sb("AB", [128, 4096])
    NT = sb("NT", [128, 128])
    NTT = sb("NTT", [128, 128])
    DT = sb("DT", [128, 128])
    XS = sb("XS", [128, 128])
    BTl = sb("BTl", [128, 128])
    KW = sb("KW", [128, 64])
    RC = sb("RC", [128, 128])
    CMP = sb("CMPT", [128, 256])
    SST = sb("SST", [128, 512])
    TM = sb("TM", [128, 48])
    EGT = sb("EGT", [128, 16])
    BET = sb("BET", [16, 1024])
    Gg = sb("Gg", [16, 1024])
    ZR = sb("ZR", [16, 128])
    KRS = sb("KRS", [16, 128])
    ROW0 = sb("ROW0", [1, 1536])
    NWC = sb("NWCs", [128, 8])
    CMBTOK = sb("CMBTOK", [128, 128])
    CB = sb("CBs", [128, 512])
    WBETS = sb("WBETS", [128, 128])
    WFS = sb("WFS", [128, 128])
    GWS = sb("GWS", [128, 128])
    IDE = sb("IDE", [128, 128])
    MAS = sb("MAS", [128, 128])
    SEK = sb("SEK", [128, 128])
    REP = sb("REPs", [16, 1024])
    ON1 = sb("ON1", [1, 128])
    OEB = sb("OEB", [128, 2])
    EBB = sb("EBB", [128, 16])
    CONSTC = sb("CONSTC", [128, 4])
    VDUM = sb("VDUM", [1, 4])

    PS = [es.enter_context(nc.psum_tensor(f"PS{i}", [128, 512], F32))
          for i in range(8)]

    p = Prog()

    bw = [None] * 8
    brd = [[] for _ in range(8)]

    def bank_deps(i):
        return brd[i] + [bw[i]]

    def bank_write(i, tok):
        bw[i] = tok
        brd[i] = []

    def bank_read(i, tok):
        brd[i].append(tok)
        return tok

    def clear_banks():
        for i in range(8):
            bw[i] = None
            brd[i] = []

    def T_mm(bank, psAP, pairs, deps, first=True, last=True):
        def fn(t, psAP=psAP, pairs=tuple(pairs), first=first, last=last):
            lastinst = None
            n = len(pairs)
            for i, (lh, rh) in enumerate(pairs):
                lastinst = t.matmul(psAP, lh, rh,
                                    start=(first and i == 0),
                                    stop=(last and i == n - 1))
            return lastinst
        dd = list(deps)
        if first:
            dd += bank_deps(bank)
        tok = p.em("tensor", fn, dd)
        bank_write(bank, tok)
        return tok

    def T_tr(bank, psAP, inAP, idAP, deps):
        tok = p.em("tensor",
                   lambda t, o=psAP, a=inAP, b=idAP: t.transpose(o, a, b),
                   list(deps) + bank_deps(bank))
        bank_write(bank, tok)
        return tok

    def V(fn, deps, reads=()):
        tok = p.em("vector", fn, deps)
        for b in reads:
            bank_read(b, tok)
        return tok

    def A(fn, deps, reads=()):
        tok = p.em("scalar", fn, deps)
        for b in reads:
            bank_read(b, tok)
        return tok

    def LD(fn, deps=()):
        return p.dma("sync", fn, deps)

    def ST(fn, deps=()):
        return p.dma("scalar", fn, deps)

    tC = {}
    tC["ide"] = LD(lambda s: s.dma_start(out=IDE[:, :], in_=identT.ap()[:, :]))
    tC["mas"] = LD(lambda s: s.dma_start(out=MAS[:, :], in_=maskiT.ap()[:, :]))
    tC["sek"] = LD(lambda s: s.dma_start(out=SEK[:, :], in_=selkT.ap()[:, :]))
    tC["rep"] = LD(lambda s: s.dma_start(out=REP[0:16, :], in_=repT.ap()[:, :]))
    tC["on1"] = LD(lambda s: s.dma_start(out=ON1[:, :], in_=ones1T.ap()[:, :]))
    tC["bet"] = LD(lambda s: s.dma_start(
        out=WBETS[:, :].rearrange("p (c f) -> p c f", c=8),
        in_=wbet.rearrange("(c p) f -> p c f", p=128)))
    tC["wf"] = LD(lambda s: s.dma_start(
        out=WFS[:, :].rearrange("p (c f) -> p c f", c=8),
        in_=wf.rearrange("(c p) f -> p c f", p=128)))
    tC["gw"] = LD(lambda s: s.dma_start(
        out=GWS[:, :].rearrange("p (c f) -> p c f", c=8),
        in_=gwt.rearrange("(c p) f -> p c f", p=128)))
    tC["nwc"] = LD(lambda s: s.dma_start(out=NWC[:, :], in_=nwc[:, :]))
    tC["oeb"] = LD(lambda s: s.dma_start(out=OEB[:, :], in_=oneb[:, :]))
    tC["ebr"] = LD(lambda s: s.dma_start(out=EBB[:, :], in_=ebrow[:, :]))
    tC["cc"] = LD(lambda s: s.dma_start(out=CONSTC[:, :], in_=constcT.ap()[:, :]))
    tz1 = V(lambda v: v.memset(SST[:, :], 0.0), [])
    tz2 = V(lambda v: v.memset(ZR[:, :], 0.0), [])
    tb2 = tC["ebr"]
    for rep in range(3):
        V(lambda v: v.memset(CMP[:, 0:256], 0.5), [])
        V(lambda v: v.tensor_copy(CMP[:, 128:144], CMP[:, 0:16]), [])
        V(lambda v: v.tensor_tensor(CMP[:, 144:160], CMP[:, 0:16],
                                    CMP[:, 16:32], OP.subtract), [])
        V(lambda v: v.tensor_tensor(CMP[:, 144:160], CMP[:, 0:16],
                                    CMP[:, 16:32], OP.mult), [])
        V(lambda v: v.tensor_tensor(CMP[:, 144:160], CMP[:, 0:16],
                                    CMP[:, 16:32], OP.add), [])
        V(lambda v: v.tensor_scalar(CMP[:, 160:176], CMP[:, 0:16],
                                    CMP[:, 0:1], None, OP.is_gt), [])
        V(lambda v: v.tensor_scalar(CMP[:, 160:176], CMP[:, 0:16], 7.5,
                                    None, OP.is_lt), [])
        V(lambda v: v.tensor_scalar(CMP[:, 160:176], CMP[:, 0:16],
                                    CMP[:, 0:1], 2.826, OP.mult, OP.mult), [])
        V(lambda v: v.tensor_scalar(CMP[:, 160:176], CMP[:, 0:16],
                                    CMP[:, 0:1], None, OP.add), [])
        V(lambda v: v.tensor_scalar(CMP[:, 160:176], CMP[:, 0:16], 2.0,
                                    None, OP.mult), [])
        V(lambda v: v.tensor_reduce(CMP[:, 176:177].unsqueeze(2),
                                    CMP[:, 0:256].rearrange(
                                        "p (a b) -> p a b", a=16)[:, 0:1],
                                    AX.X, OP.add), [])
        V(lambda v: v.tensor_reduce(CMP[:, 176:192].unsqueeze(2),
                                    CMP[:, 0:256].rearrange(
                                        "p (a b) -> p a b", a=16),
                                    AX.X, OP.add), [])
        V(lambda v: v.reciprocal(CMP[:, 192:193], CMP[:, 0:1]), [])
        V(lambda v: v.scalar_tensor_tensor(
            CMP[:, 193:209], CMP[:, 0:16], CMP[:, 0:1], CMP[:, 16:32],
            OP.mult, OP.subtract), [])
        V(lambda v: v.scalar_tensor_tensor(
            CMP[:, 193:209], CMP[:, 0:16], CMP[:, 0:1], CMP[:, 16:32],
            OP.mult, OP.mult), [])
        V(lambda v: v.tensor_scalar(CMP[:, 209:225], CMP[:, 0:16],
                                    CMP[:, 0:1], None, OP.mult), [])
        V(lambda v: v.tensor_tensor_scan(CMP[:, 225:241], CMP[:, 0:16],
                                         CMP[:, 16:32], 0.0, OP.add,
                                         OP.add), [])
        V(lambda v: v.tensor_tensor(CMP[:, 144:256], CMP[:, 0:112],
                                    CMP[:, 16:128], OP.mult), [])
    p.barrier()
    clear_banks()

    wslot_cons = {i: None for i in range(6)}

    def load_matrix(srcap, pair, deps=()):
        toks = []
        if srcap.ndim == 3:
            srcap = srcap.squeeze(0)
        src3 = srcap.rearrange("(c p) f -> p c f", p=128)
        for k, sl in enumerate(pair):
            dst3 = W[sl][:, :].rearrange("p (c f) -> p c f", c=8)
            s3 = src3[:, :, k * 512:(k + 1) * 512]
            toks.append(LD(lambda s, d=dst3, sr=s3: s.dma_start(out=d, in_=sr),
                           deps=list(deps) + [wslot_cons[sl]]))
        return toks

    def wslotap(pair, cin, m):
        sl = pair[m // 4]
        o = cin * 512 + (m % 4) * 128
        return W[sl][:, o:o + 128]

    xsrc = xt.rearrange("(c p) t -> c p t", p=128)
    osrc = out.rearrange("(c p) t -> c p t", p=128)

    last_xn_reader = None
    prev_stores = []

    for H in range(N_HALVES):
        hoff = H * HTOK

        tx = []
        for c in range(8):
            tx.append(LD(lambda s, c=c, hoff=hoff: s.dma_start(
                out=XN[:, c * 1024:(c + 1) * 1024],
                in_=xsrc[c][:, hoff:hoff + 1024]),
                deps=[last_xn_reader]))
        if DEBUG_TAP == "xnboth" and H == 1:
            dsrcB = dbg.rearrange("(c p) t -> c p t", p=128)
            tpre = [ST(lambda g, c=c: g.dma_start(
                out=dsrcB[c][:, 0:1024],
                in_=XN[:, c * 1024:(c + 1) * 1024]), deps=tx)
                for c in range(8)]
            p.em("gpsimd", lambda g: None, tpre)
            p.barrier()
        if DEBUG_TAP == "xnpre":
            dsrc0 = dbg.rearrange("(c p) t -> c p t", p=128)
            for c in range(8):
                ST(lambda g, c=c, hoff=hoff: g.dma_start(
                    out=dsrc0[c][:, hoff:hoff + 1024],
                    in_=XN[:, c * 1024:(c + 1) * 1024]), deps=tx)
            p.barrier()
        for ch in range(2):
            co = ch * 512
            tsx = T_mm(0, PS[0][0:1, 0:512],
                       [(OEB[:, 0:1],
                         XN[:, c * 1024 + co:c * 1024 + co + 512])
                        for c in range(8)],
                       deps=tx + [tC["oeb"]])
            tsq = []
            for c in range(8):
                scr = AB[:, (c % 2) * 512:(c % 2) * 512 + 512]
                tq = A(lambda s, sc=scr, c=c, co=co: s.activation(
                    sc, XN[:, c * 1024 + co:c * 1024 + co + 512], AF.Square),
                    [tx[c]] + ([tsq[c - 2]] if c >= 2 else []) + prev_stores)
                tm = T_mm(1, PS[1][0:1, 0:512], [(OEB[:, 0:1], scr)],
                          deps=[tq, tC["oeb"]], first=(c == 0), last=(c == 7))
                tsq.append(tm)
            t1 = V(lambda v: v.tensor_scalar(
                ROW0[0:1, 0:512], PS[0][0:1, 0:512], 1.0 / 1024.0, None,
                OP.mult), [tsx], reads=[0])
            t2 = V(lambda v: v.tensor_scalar(
                ROW0[0:1, 1024:1536], PS[1][0:1, 0:512], 1.0 / 1024.0, None,
                OP.mult), [tsq[-1]], reads=[1])
            t3 = V(lambda v: v.tensor_tensor(
                PS[2][0:1, 0:512], ROW0[0:1, 0:512], ROW0[0:1, 0:512],
                OP.mult), [t1] + bank_deps(2))
            bank_write(2, t3)
            t4 = V(lambda v: v.tensor_tensor(
                PS[1][0:1, 0:512], ROW0[0:1, 1024:1536], PS[2][0:1, 0:512],
                OP.subtract), [t2, t3] + bank_deps(1), reads=[2])
            bank_write(1, t4)
            t5 = A(lambda s: s.activation(PS[1][0:1, 0:512], PS[1][0:1, 0:512],
                                          AF.Sqrt, bias=CONSTC[0:1, 0:1]),
                   [t4, tC["cc"]])
            bank_write(1, t5)
            t6 = V(lambda v: v.reciprocal(ROW0[0:1, 512:1024],
                                          PS[1][0:1, 0:512]), [t5], reads=[1])
            tb_m = T_mm(3, PS[3][:, 0:512], [(ON1[0:1, :], ROW0[0:1, 0:512])],
                        deps=[t1, tC["on1"]])
            tb_r = T_mm(4, PS[4][:, 0:512], [(ON1[0:1, :],
                                              ROW0[0:1, 512:1024])],
                        deps=[t6])
            for c in range(8):
                ta = V(lambda v, c=c, co=co: v.tensor_tensor(
                    XN[:, c * 1024 + co:c * 1024 + co + 512],
                    XN[:, c * 1024 + co:c * 1024 + co + 512],
                    PS[3][:, 0:512], OP.subtract),
                    [tb_m, tsx, tsq[-1]], reads=[3])
                ta = V(lambda v, c=c, co=co: v.tensor_tensor(
                    XN[:, c * 1024 + co:c * 1024 + co + 512],
                    XN[:, c * 1024 + co:c * 1024 + co + 512],
                    PS[4][:, 0:512], OP.mult), [tb_r, ta], reads=[4])
        p.barrier()
        clear_banks()
        if DEBUG_TAP == "xnboth" and H == 1:
            dsrcC = dbg.rearrange("(c p) t -> c p t", p=128)
            tpost = [ST(lambda g, c=c: g.dma_start(
                out=dsrcC[c][:, 1024:2048],
                in_=XN[:, c * 1024:(c + 1) * 1024])) for c in range(8)]
            p.em("gpsimd", lambda g: None, tpost)
            p.barrier()
        if DEBUG_TAP == "xn" and (TAP_HALF is None or TAP_HALF == H):
            dsrc = dbg.rearrange("(c p) t -> c p t", p=128)
            tstap = [ST(lambda g, c=c, hoff=hoff: g.dma_start(
                out=dsrc[c][:, hoff:hoff + 1024],
                in_=XN[:, c * 1024:(c + 1) * 1024])) for c in range(8)]
            p.em("gpsimd", lambda g: None, tstap)
            p.barrier()

        if "rows" in SKIP:
            last_xn_reader = T_mm(0, PS[0][0:1, 0:512],
                                  [(OEB[:, 0:1], XN[:, 0:512])],
                                  deps=tx + [tC["oeb"]])
            p.barrier()
            clear_banks()
            continue
        rowtok = {}
        for name, wsb, b0 in (("bet", WBETS, 0), ("f", WFS, 2),
                              ("gate", GWS, 4)):
            for ch in range(2):
                co = ch * 512
                bank = b0 + ch
                tg = T_mm(bank, PS[bank][0:16, 0:512],
                          [(wsb[:, c * 16:(c + 1) * 16],
                            XN[:, c * 1024 + co:c * 1024 + co + 512])
                           for c in range(8)], deps=[tC["bet"], tC["wf"],
                                                     tC["gw"]])
                rowtok[(name, ch)] = tg
        ta1 = A(lambda s: s.activation(BET[:, 0:512], PS[2][0:16, 0:512],
                                       AF.Sigmoid), [rowtok[("f", 0)]],
                reads=[2])
        ta2 = A(lambda s: s.activation(BET[:, 512:1024],
                                       PS[3][0:16, 0:512], AF.Sigmoid),
                [rowtok[("f", 1)]], reads=[3])
        tla = A(lambda s: s.activation(BET[:, :], BET[:, :], AF.Ln),
                [ta1, ta2])
        tgs = []
        for c in range(8):
            tgs.append(V(lambda v, c=c: v.tensor_tensor_scan(
                Gg[:, c * 128:(c + 1) * 128],
                BET[:, c * 128:(c + 1) * 128], ZR[:, :], 0.0,
                OP.add, OP.add), [tla, tz2]))
        tbb1 = A(lambda s: s.activation(BET[:, 0:512], PS[0][0:16, 0:512],
                                        AF.Sigmoid),
                 [rowtok[("bet", 0)]] + tgs, reads=[0])
        tbb2 = A(lambda s: s.activation(BET[:, 512:1024],
                                        PS[1][0:16, 0:512], AF.Sigmoid),
                 [rowtok[("bet", 1)]] + tgs, reads=[1])
        tbb = V(lambda v: v.tensor_scalar(BET[:, :], BET[:, :], 2.0,
                                          None, OP.mult), [tbb1, tbb2])
        tsl1 = A(lambda s: s.activation(AB[0:16, 0:512], PS[4][0:16, 0:512],
                                        AF.Sigmoid), [rowtok[("gate", 0)]],
                 reads=[4])
        tsl2 = A(lambda s: s.activation(AB[0:16, 512:1024],
                                        PS[5][0:16, 0:512], AF.Sigmoid),
                 [rowtok[("gate", 1)]], reads=[5])
        tsl = V(lambda v: v.tensor_scalar(AB[0:16, 0:1024], AB[0:16, 0:1024],
                                          OEB[0:16, 1:2], None, OP.add),
                [tsl1, tsl2, tC["oeb"]])
        last_xn_reader = rowtok[("gate", 1)]
        p.barrier()
        clear_banks()
        if DEBUG_TAP == "rows":
            ST(lambda g, hoff=hoff: g.dma_start(
                out=dbg[0:16, hoff:hoff + 1024], in_=BET[:, :]))
            ST(lambda g, hoff=hoff: g.dma_start(
                out=dbg[16:32, hoff:hoff + 1024], in_=Gg[:, :]))
            p.barrier()

        cmbtoks = {}
        if "gating" in SKIP:
            p.barrier()
            clear_banks()
            continue
        pt = {}
        ch = [None]

        def SEQ(tok):
            ch[0] = tok
            return tok

        for t8 in range(8):
            tt = SEQ(T_tr(0, PS[0][:, 0:16],
                          AB[0:16, t8 * 128:(t8 + 1) * 128],
                          IDE[0:16, 0:16], [tsl, tC["ide"], ch[0]]))
            te = SEQ(A(lambda s: s.copy(NT[:, 0:16], PS[0][:, 0:16]),
                       [tt], reads=[0]))
            a_out = CMP[:, 0:256].rearrange("p (a b) -> p a b", a=16)
            for e in range(16):
                SEQ(V(lambda v, e=e: v.tensor_scalar(
                    CMP[:, e * 16:(e + 1) * 16], NT[:, 0:16],
                    NT[:, e:e + 1], None, OP.is_gt), [ch[0]]))
            tc2 = SEQ(V(lambda v, ao=a_out: v.tensor_reduce(
                NT[:, 16:32].unsqueeze(2), ao, AX.X, OP.add), [ch[0]]))
            tb3 = SEQ(A(lambda s: s.copy(NT[:, 80:96], NT[:, 16:32]),
                        [ch[0]]))
            tc3 = SEQ(V(lambda v: v.tensor_scalar(
                NT[:, 32:48], NT[:, 80:96], 7.5, None, OP.is_lt), [ch[0]]))
            tc4 = SEQ(V(lambda v: v.tensor_tensor(
                NT[:, 48:64], NT[:, 0:16], EBB[:, :], OP.subtract),
                [ch[0], tb2]))
            SEQ(A(lambda s: s.copy(NTT[:, 96:112], NT[:, 32:48]), [ch[0]]))
            SEQ(A(lambda s: s.copy(NTT[:, 112:128], NT[:, 48:64]), [ch[0]]))
            tc5 = SEQ(V(lambda v: v.tensor_tensor(
                NTT[:, 0:16], NTT[:, 96:112], NTT[:, 112:128], OP.mult),
                [ch[0]]))
            tb4 = SEQ(A(lambda s: s.copy(NT[:, 96:112], NTT[:, 0:16]),
                        [ch[0]]))
            tc6 = SEQ(V(lambda v: v.tensor_reduce(
                NT[:, 64:65].unsqueeze(2), NT[:, 96:112].unsqueeze(1),
                AX.X, OP.add), [ch[0]]))
            tb5 = SEQ(A(lambda s: s.copy(NT[:, 66:67], NT[:, 64:65]),
                        [ch[0]]))
            tc7 = SEQ(V(lambda v: v.reciprocal(NT[:, 65:66], NT[:, 66:67]),
                        [ch[0]]))
            tb6 = SEQ(A(lambda s: s.copy(NT[:, 67:68], NT[:, 65:66]),
                        [ch[0]]))
            tc8 = SEQ(V(lambda v, t8=t8: v.tensor_scalar(
                CMBTOK[:, t8 * 16:(t8 + 1) * 16], NT[:, 96:112],
                NT[:, 67:68], 2.826, OP.mult, OP.mult), [ch[0]]))
            cmbtoks[t8] = tc8
            pt = {}
            if DEBUG_TAP == "gating" and t8 == 0:
                tdn = ST(lambda g, H=H: g.dma_start(
                    out=dbg[:, H * 2048 + 128:H * 2048 + 256],
                    in_=NT[:, 0:128]), deps=[tc8])
                tdn2 = ST(lambda g, H=H: g.dma_start(
                    out=dbg[:, H * 2048 + 1280:H * 2048 + 1376],
                    in_=NTT[:, 0:96]), deps=[tc8])
                p.em("gpsimd", lambda g: None, [tdn, tdn2])
                p.barrier()
        p.barrier()
        clear_banks()
        if DEBUG_TAP == "gating":
            t_g1 = ST(lambda g, H=H: g.dma_start(
                out=dbg[:, H * 2048:H * 2048 + 128], in_=CMBTOK[:, :]))
            t_g2 = ST(lambda g, H=H: g.dma_start(
                out=dbg[0:16, H * 2048 + 256:H * 2048 + 1280],
                in_=AB[0:16, 0:1024]))
            p.em("gpsimd", lambda g: None, [t_g1, t_g2])
            p.barrier()

        acc_guard = list(prev_stores)
        if "attn" not in SKIP:
            lseq = [0]

            def attn_load(srcap):
                pair = (2 * (lseq[0] % 3), 2 * (lseq[0] % 3) + 1)
                toks = load_matrix(srcap, pair)
                lseq[0] += 1
                return pair, toks

            for c16 in range(N_C16):
                tkoff = c16 * 128
                lastmm = {}
                wsrc = {0: wq, 1024: wk, 2048: wv}
                pair_of = {}
                tl_of = {}
                for aoff in (0, 1024, 2048):
                    pair_of[aoff], tl_of[aoff] = attn_load(wsrc[aoff])
                    pair, tl, act = (pair_of[aoff], tl_of[aoff],
                                     aoff in (0, 1024))
                    for m in range(8):
                        bank = (m % 2) + {0: 0, 1024: 2, 2048: 4}[aoff]
                        tmm = T_mm(bank, PS[bank][:, 0:128],
                                   [(wslotap(pair, cin, m),
                                     XN[:, cin * 1024 + tkoff:
                                        cin * 1024 + tkoff + 128])
                                    for cin in range(8)],
                                   deps=tl)
                        dst = AB[:, aoff + m * 128:aoff + m * 128 + 128]
                        if act:
                            tev = A(lambda s, d=dst, b=bank: s.activation(
                                d, PS[b][:, 0:128], AF.Silu), [tmm],
                                reads=[bank])
                        else:
                            tev = A(lambda s, d=dst, b=bank: s.copy(
                                d, PS[b][:, 0:128]), [tmm], reads=[bank])
                        lastmm[(aoff, m)] = tev
                    for sl in pair:
                        wslot_cons[sl] = lastmm[(aoff, 7)]
                po_, tlo = attn_load(wo)
                tqkv = [lastmm[(a, m)] for a in (0, 1024, 2048)
                        for m in range(8)]
                tknmm = []
                for cin in range(8):
                    scr = CMP[:, (cin % 2) * 128:(cin % 2) * 128 + 128]
                    tq = A(lambda s, sc=scr, cin=cin: s.activation(
                        sc, AB[:, 1024 + cin * 128:1024 + cin * 128 + 128],
                        AF.Square),
                        [lastmm[(1024, cin)]]
                        + ([tknmm[cin - 2]] if cin >= 2 else []))
                    tm2 = T_mm(6, PS[6][0:16, 0:128],
                               [(SEK[:, cin * 16:(cin + 1) * 16], scr)],
                               deps=[tq, tC["sek"]],
                               first=(cin == 0), last=(cin == 7))
                    tknmm.append(tm2)
                trk1 = A(lambda s: s.activation(KRS[:, :], PS[6][0:16, 0:128],
                                                AF.Sqrt,
                                                bias=CONSTC[0:16, 1:2]),
                         [tknmm[-1], tC["cc"]], reads=[6])
                trk2 = V(lambda v: v.reciprocal(KRS[:, :], KRS[:, :]), [trk1])
                tkn = []
                for cin in range(8):
                    tbc = T_mm(7, PS[7][:, 0:128],
                               [(REP[0:16, cin * 128:(cin + 1) * 128],
                                 KRS[:, :])], deps=[trk2, tC["rep"]])
                    tmu = V(lambda v, cin=cin: v.tensor_tensor(
                        AB[:, 1024 + cin * 128:1024 + cin * 128 + 128],
                        AB[:, 1024 + cin * 128:1024 + cin * 128 + 128],
                        PS[7][:, 0:128], OP.mult),
                        [tbc, lastmm[(1024, cin)]], reads=[7])
                    tkn.append(tmu)
                if DEBUG_TAP == "qkv" and H == 0 and c16 == 0:
                    p.barrier()
                    ST(lambda g: g.dma_start(out=dbg[:, 0:4096], in_=AB[:, :]))
                    ST(lambda g: g.dma_start(out=dbg[:, 4096:8192],
                                             in_=W[0][:, :]))
                    ST(lambda g: g.dma_start(out=dbg[:, 8192:12288],
                                             in_=XN[:, 0:4096]))
                    p.barrier()
                    clear_banks()
                ttma = T_tr(5, PS[5][:, 0:16],
                            BET[:, tkoff:tkoff + 128],
                            IDE[0:16, 0:16], [tbb] + tgs)
                ttmb = p.em("tensor", lambda t, tkoff=tkoff: t.transpose(
                    PS[5][:, 16:32], Gg[:, tkoff:tkoff + 128],
                    IDE[0:16, 0:16]), [ttma] + tgs)
                bank_write(5, ttmb)
                ttmc = A(lambda s: s.copy(TM[:, 0:16], PS[5][:, 0:16]),
                         [ttma, ttmb], reads=[5])
                ttmd = A(lambda s: s.copy(TM[:, 32:48], PS[5][:, 16:32]),
                         [ttmb], reads=[5])
                ttme = A(lambda s: s.activation(TM[:, 16:32], TM[:, 0:16],
                                                AF.Ln), [ttmc])
                ttm2v = V(lambda v: v.tensor_tensor(TM[:, 16:32],
                                                    TM[:, 16:32],
                                                    TM[:, 32:48],
                                                    OP.subtract),
                          [ttme, ttmd])
                ttm2 = A(lambda s: s.copy(TM[:, 16:32], TM[:, 16:32]),
                         [ttm2v])
                tegt = A(lambda s: s.activation(EGT[:, :], TM[:, 32:48],
                                                AF.Exp), [ttm2, ttmd])

                BKA, BKB, BKKV, BKU, BKQ1, BKQ2, BKWO, BKG = range(8)
                totev = {}
                tNTT_free = None
                tprev = {}
                for h in range(NH):
                    dc, po2 = h // 2, 64 * (h % 2)
                    q_h = AB[po2:po2 + 64, dc * 128:dc * 128 + 128]
                    k_h = AB[po2:po2 + 64,
                             1024 + dc * 128:1024 + dc * 128 + 128]
                    v_h = AB[po2:po2 + 64,
                             2048 + dc * 128:2048 + dc * 128 + 128]
                    s_h = SST[po2:po2 + 64, dc * 64:dc * 64 + 64]
                    tg1 = T_tr(BKG, PS[BKG][0:1, 0:128], TM[:, 32 + h:33 + h],
                               IDE[:, :], [ttm2, ttmd])
                    tg2 = A(lambda s: s.copy(ROW0[0:1, 1024:1152],
                                             PS[BKG][0:1, 0:128]), [tg1],
                            reads=[BKG])
                    tg3 = T_mm(BKB, PS[BKB][:, 0:128],
                               [(ON1[0:1, :], ROW0[0:1, 1024:1152])],
                               deps=[tg2])
                    tkkt = T_mm(BKA, PS[BKA][:, 0:128], [(k_h, k_h)],
                                deps=tkn)
                    td1 = V(lambda v, h=h: v.tensor_scalar(
                        DT[:, :], PS[BKB][:, 0:128], TM[:, 16 + h:17 + h],
                        None, OP.add), [tg3, ttm2], reads=[BKB])
                    td2 = V(lambda v: v.tensor_tensor(
                        DT[:, :], DT[:, :], MAS[:, :], OP.add),
                        [td1, tC["mas"]])
                    td3 = A(lambda s: s.activation(DT[:, :], DT[:, :],
                                                   AF.Exp), [td2])
                    tn1 = V(lambda v: v.tensor_tensor(
                        NTT[:, :], PS[BKA][:, 0:128], DT[:, :], OP.mult),
                        [tkkt, td3, tNTT_free], reads=[BKA])
                    tn1b = A(lambda s: s.copy(XS[:, :], NTT[:, :]),
                             [tn1, tprev.get("xs_last")])
                    tn2 = V(lambda v, h=h: v.scalar_tensor_tensor(
                        NT[:, :], IDE[:, :], TM[:, h:h + 1], XS[:, :],
                        OP.mult, OP.subtract), [tn1b, tC["ide"], ttm2])
                    tkt = T_tr(BKKV, PS[BKKV][:, 0:64], k_h,
                               IDE[po2:po2 + 64, po2:po2 + 64], tkn)
                    tvt = p.em("tensor", lambda t, v_h=v_h, po2=po2:
                               t.transpose(PS[BKKV][:, 64:128], v_h,
                                           IDE[po2:po2 + 64, po2:po2 + 64]),
                               [lastmm[(2048, dc)], tkt])
                    bank_write(BKKV, tvt)
                    tx0 = V(lambda v, h=h: v.tensor_scalar(
                        XS[:, 0:64], PS[BKKV][:, 0:64], EGT[:, h:h + 1],
                        None, OP.mult), [tkt, tvt, tegt, tn2], reads=[BKKV])
                    tx1 = A(lambda s: s.copy(XS[:, 64:128],
                                             PS[BKKV][:, 64:128]),
                            [tvt, tn2], reads=[BKKV])
                    tkw1 = V(lambda v, h=h: v.tensor_tensor(
                        RC[:, 66:67], PS[BKB][:, 127:128],
                        TM[:, 16 + h:17 + h], OP.add), [tg3, ttm2],
                        reads=[BKB])
                    tkw2 = A(lambda s: s.activation(RC[:, 66:67],
                                                    RC[:, 66:67], AF.Exp),
                             [tkw1])
                    tkw = V(lambda v: v.tensor_scalar(
                        KW[:, 0:64], PS[BKKV][:, 0:64], RC[:, 66:67], None,
                        OP.mult), [tkw2, tkt, tvt], reads=[BKKV])
                    tnt1 = T_tr(BKQ1, PS[BKQ1][:, 0:128], NT[:, :], IDE[:, :],
                                [tn2])
                    tNTT = A(lambda s: s.copy(NTT[:, :], PS[BKQ1][:, 0:128]),
                             [tnt1], reads=[BKQ1])
                    tNT = tn2
                    txcur = tx0
                    txcur2 = tx1
                    for j in range(7):
                        tu = T_mm(BKU, PS[BKU][:, 0:128],
                                  [(IDE[:, :], XS[:, :]),
                                   (NT[:, :], XS[:, :])],
                                  deps=[tNT, txcur, txcur2, tC["ide"]])
                        if j < 6:
                            tq1 = T_mm(BKQ1, PS[BKQ1][:, 0:128],
                                       [(NTT[:, :], NT[:, :])],
                                       deps=[tNT, tNTT, tn1b])
                            tq2 = T_mm(BKQ2, PS[BKQ2][:, 0:128],
                                       [(NT[:, :], NTT[:, :])],
                                       deps=[tNT, tNTT])
                        txcur = A(lambda s: s.copy(XS[:, :],
                                                   PS[BKU][:, 0:128]),
                                  [tu], reads=[BKU])
                        txcur2 = txcur
                        if j < 6:
                            tNT = A(lambda s: s.copy(NT[:, :],
                                                     PS[BKQ1][:, 0:128]),
                                    [tq1, tq2], reads=[BKQ1])
                            tNTT = A(lambda s: s.copy(NTT[:, :],
                                                      PS[BKQ2][:, 0:128]),
                                     [tq2], reads=[BKQ2])
                    tkqt = T_mm(BKA, PS[BKA][:, 0:128], [(k_h, q_h)],
                                deps=tkn + [lastmm[(0, dc)]])
                    tbt = V(lambda v: v.tensor_tensor(
                        BTl[:, :], PS[BKA][:, 0:128], DT[:, :], OP.mult),
                        [tkqt, td3], reads=[BKA])
                    tpw = T_mm(BKWO, PS[BKWO][po2:po2 + 64, 0:128],
                               [(XS[:, 0:64], BTl[:, :])],
                               deps=[txcur, tbt])
                    tt2 = A(lambda s, po2=po2: s.activation(
                        DT[po2:po2 + 64, :], PS[BKB][po2:po2 + 64, 0:128],
                        AF.Exp), [tg3, tbt, tn1], reads=[BKB])
                    tt3 = V(lambda v, q_h=q_h, po2=po2: v.tensor_tensor(
                        DT[po2:po2 + 64, :], DT[po2:po2 + 64, :], q_h,
                        OP.mult), [tt2, lastmm[(0, dc)]])
                    tt3b = A(lambda s, po2=po2: s.copy(
                        DT[po2:po2 + 64, :], DT[po2:po2 + 64, :]), [tt3])
                    tqe = V(lambda v, po2=po2: v.tensor_tensor(
                        NTT[po2:po2 + 64, :], DT[po2:po2 + 64, :],
                        PS[BKWO][po2:po2 + 64, 0:128],
                        OP.subtract), [tt3b, tpw, tNTT], reads=[BKWO])
                    to1 = T_mm(BKWO, PS[BKWO][:, 0:64],
                               [(BTl[:, :], XS[:, 64:128]),
                                (NTT[po2:po2 + 64, :], s_h)],
                               deps=[tbt, txcur, tqe])
                    tgt = T_mm(BKG, PS[BKG][po2:po2 + 64, 0:64],
                               [(XS[:, 0:64], KW[:, 0:64])],
                               deps=[txcur, tkw, tg2])
                    tbc_ = A(lambda s, po2=po2: s.activation(
                        RC[po2:po2 + 64, 67:68],
                        PS[BKB][po2:po2 + 64, 127:128], AF.Exp),
                        [tg3, tt2], reads=[BKB])
                    tft = V(lambda v, po2=po2: v.scalar_tensor_tensor(
                        RC[po2:po2 + 64, 0:64],
                        IDE[po2:po2 + 64, po2:po2 + 64],
                        RC[po2:po2 + 64, 67:68],
                        PS[BKG][po2:po2 + 64, 0:64], OP.mult, OP.subtract),
                        [tgt, tbc_, tC["ide"]], reads=[BKG])
                    ts12 = T_mm(BKKV, PS[BKKV][po2:po2 + 64, 0:64],
                                [(RC[po2:po2 + 64, 0:64], s_h),
                                 (KW[:, 0:64], XS[:, 64:128])],
                                deps=[tft, tkw, txcur, to1, tx0, tx1])
                    tsev = V(lambda v, s_h=s_h, po2=po2: v.tensor_copy(
                        s_h, PS[BKKV][po2:po2 + 64, 0:64]), [ts12],
                        reads=[BKKV])
                    tsq_ = A(lambda s: s.activation(DT[:, 0:64],
                                                    PS[BKWO][:, 0:64],
                                                    AF.Square),
                             [to1, tt3], reads=[BKWO])
                    trd = V(lambda v: v.tensor_reduce(
                        RC[:, 64:65].unsqueeze(2), DT[:, 0:64].unsqueeze(1),
                        AX.X, OP.add), [tsq_])
                    trs = A(lambda s: s.activation(RC[:, 65:66], RC[:, 64:65],
                                                   AF.Sqrt,
                                                   bias=CONSTC[:, 2:3],
                                                   scale=CONSTC[:, 3:4]),
                            [trd, tC["cc"]])
                    trr = V(lambda v: v.reciprocal(RC[:, 65:66],
                                                   RC[:, 65:66]), [trs])
                    trb = A(lambda s: s.copy(RC[:, 68:69], RC[:, 65:66]),
                            [trr])
                    ton = V(lambda v: v.tensor_scalar(
                        DT[:, 0:64], PS[BKWO][:, 0:64], RC[:, 68:69], None,
                        OP.mult), [trb, tsq_], reads=[BKWO])
                    tot = T_tr(BKQ2, PS[BKQ2][0:64, 0:128],
                               DT[:, 0:64], IDE[:, :], [ton])
                    tote = A(lambda s, dc=dc, po2=po2: s.copy(
                        AB[po2:po2 + 64,
                           3072 + dc * 128:3072 + dc * 128 + 128],
                        PS[BKQ2][0:64, 0:128]), [tot], reads=[BKQ2])
                    totev[h] = tote
                    tNTT_free = to1
                    tprev["xs_last"] = ts12

                tnw = []
                for cin in range(8):
                    tn_ = V(lambda v, cin=cin: v.tensor_scalar(
                        AB[:, 3072 + cin * 128:3072 + cin * 128 + 128],
                        AB[:, 3072 + cin * 128:3072 + cin * 128 + 128],
                        NWC[:, cin:cin + 1], None, OP.mult),
                        [totev[2 * cin], totev[2 * cin + 1], tC["nwc"]])
                    tnw.append(tn_)
                for m in range(8):
                    bank = m % 2
                    tmm = T_mm(bank, PS[bank][:, 0:128],
                               [(wslotap(po_, cin, m),
                                 AB[:, 3072 + cin * 128:
                                    3072 + cin * 128 + 128])
                                for cin in range(8)],
                               deps=tnw + tlo)
                    tev = V(lambda v, m=m, b=bank, tkoff=tkoff: v.tensor_copy(
                        ACC[:, m * 1024 + tkoff:m * 1024 + tkoff + 128],
                        PS[b][:, 0:128]), [tmm] + acc_guard, reads=[bank])
                    if m == 7:
                        for sl in po_:
                            wslot_cons[sl] = tev
                acc_guard = []
                last_xn_reader = tmm
                p.barrier()
                clear_banks()
        else:
            tzacc = []
            for c in range(8):
                tzacc.append(V(lambda v, c=c: v.memset(
                    ACC[:, c * 1024:(c + 1) * 1024], 0.0), acc_guard))
            acc_guard = []
            p.barrier()
            clear_banks()

        def ffn_block(tlg, tlu, tld, expert, e=None):
            nonlocal last_xn_reader
            tdown_prev = []
            for grp in range(2):
                hlast = []
                tu_last = None
                for i in range(8):
                    bank = i % 2
                    tg_ = T_mm(bank, PS[bank][:, 0:512],
                               [(wslotap((0, 1), cin, i),
                                 XN[:, cin * 1024 + grp * 512:
                                    cin * 1024 + grp * 512 + 512])
                                for cin in range(8)],
                               deps=tlg)
                    tsl_ = A(lambda s, i=i, b=bank: s.activation(
                        AB[:, i * 512:(i + 1) * 512], PS[b][:, 0:512],
                        AF.Silu), [tg_] + tdown_prev, reads=[bank])
                    bank2 = 2 + i % 2
                    tu_ = T_mm(bank2, PS[bank2][:, 0:512],
                               [(wslotap((2, 3), cin, i),
                                 XN[:, cin * 1024 + grp * 512:
                                    cin * 1024 + grp * 512 + 512])
                                for cin in range(8)],
                               deps=tlu)
                    tu_last = tu_
                    th_ = V(lambda v, i=i, b=bank2: v.tensor_tensor(
                        AB[:, i * 512:(i + 1) * 512],
                        AB[:, i * 512:(i + 1) * 512],
                        PS[b][:, 0:512], OP.mult), [tu_, tsl_], reads=[bank2])
                    hlast.append(th_)
                last_xn_reader = tu_last
                if grp == 1:
                    for sl in (0, 1, 2, 3):
                        wslot_cons[sl] = hlast[-1]
                if expert:
                    tcb1 = []
                    for k4 in range(4):
                        t8 = grp * 4 + k4
                        tcb1.append(T_tr(
                            4, PS[4][0:1, k4 * 128:(k4 + 1) * 128],
                            CMBTOK[:, t8 * 16 + e:t8 * 16 + e + 1],
                            IDE[:, :], [cmbtoks[t8]]))
                    tcb2 = A(lambda s: s.copy(ROW0[0:1, 1024:1536],
                                              PS[4][0:1, 0:512]), tcb1,
                             reads=[4])
                    tcb3 = T_mm(5, PS[5][:, 0:512],
                                [(ON1[0:1, :], ROW0[0:1, 1024:1536])],
                                deps=[tcb2])
                    tcb4 = A(lambda s: s.copy(CB[:, :], PS[5][:, 0:512]),
                             [tcb3], reads=[5])
                tdown_prev = []
                for m in range(8):
                    bank = 6 + m % 2
                    ty_ = T_mm(bank, PS[bank][:, 0:512],
                               [(wslotap((4, 5), i2, m),
                                 AB[:, i2 * 512:(i2 + 1) * 512])
                                for i2 in range(8)],
                               deps=tld + hlast)
                    tdown_prev.append(ty_)
                    if expert:
                        tv1 = V(lambda v, b=bank: v.tensor_tensor(
                            PS[b][:, 0:512], PS[b][:, 0:512], CB[:, :],
                            OP.mult), [ty_, tcb4])
                        bank_write(bank, tv1)
                        src_tok = tv1
                    else:
                        src_tok = ty_
                    tv2 = V(lambda v, m=m, b=bank, grp=grp: v.tensor_tensor(
                        ACC[:, m * 1024 + grp * 512:
                            m * 1024 + grp * 512 + 512],
                        ACC[:, m * 1024 + grp * 512:
                            m * 1024 + grp * 512 + 512],
                        PS[b][:, 0:512], OP.add), [src_tok], reads=[bank])
                if grp == 1:
                    for sl in (4, 5):
                        wslot_cons[sl] = tv2

        if "moe" not in SKIP:
            for e in range(N_EXPERTS):
                tlg = load_matrix(wg[e], (0, 1))
                tlu = load_matrix(wu[e], (2, 3))
                tld = load_matrix(wd[e], (4, 5))
                ffn_block(tlg, tlu, tld, expert=True, e=e)
                p.barrier()
                clear_banks()
        if "shared" not in SKIP:
            tlg = load_matrix(sg, (0, 1))
            tlu = load_matrix(su, (2, 3))
            tld = load_matrix(sd, (4, 5))
            ffn_block(tlg, tlu, tld, expert=False)
            p.barrier()
            clear_banks()

        if "store" in SKIP:
            continue
        tr = []
        for c in range(8):
            sl = c // 4
            dloc = (c % 4) * 1024
            if "reload" in SKIP:
                tl = None
            else:
                tl = LD(lambda s, c=c, sl=sl, dloc=dloc, hoff=hoff: s.dma_start(
                    out=W[sl][:, dloc:dloc + 1024],
                    in_=xsrc[c][:, hoff:hoff + 1024]),
                    deps=[wslot_cons[sl]])
            if "resid" not in SKIP:
                ta = V(lambda v, c=c, sl=sl, dloc=dloc: v.tensor_tensor(
                    ACC[:, c * 1024:(c + 1) * 1024],
                    ACC[:, c * 1024:(c + 1) * 1024],
                    W[sl][:, dloc:dloc + 1024], OP.add), [tl])
            else:
                ta = tl
            tr.append(ta)
            wslot_cons[sl] = ta
        prev_stores = []
        srcbuf = XN if "outxn" in SKIP else ACC
        if "outst" not in SKIP:
            for c in range(8):
                prev_stores.append(ST(lambda g, c=c, hoff=hoff, srcbuf=srcbuf:
                    g.dma_start(
                        out=osrc[c][:, hoff:hoff + 1024],
                        in_=srcbuf[:, c * 1024:(c + 1) * 1024]), deps=[tr[c]]))
        p.barrier()
        clear_banks()

    nst = p.ndma["scalar"]
    if nst:
        p.em("gpsimd", lambda g: None, [("d", "scalar", nst - 1)])

    body = p.emit(nc, es, vdummy=None)
    with nc.Block() as blk:
        blk.tensor(lambda e: body("tensor", e))
        blk.vector(lambda e: body("vector", e))
        blk.scalar(lambda e: body("scalar", e))
        blk.gpsimd(lambda e: body("gpsimd", e))
        blk.sync(lambda e: body("sync", e))
    es.close()
    return nc


def kernel(**inputs):
    f = np.float32
    x = np.asarray(inputs["L0_attn_x"], dtype=f)
    com = {
        "wq": np.ascontiguousarray(inputs["L0_attn_Wq"], dtype=f),
        "wk": np.ascontiguousarray(inputs["L0_attn_Wk"], dtype=f),
        "wv": np.ascontiguousarray(inputs["L0_attn_Wv"], dtype=f),
        "wo": np.ascontiguousarray(inputs["L0_attn_Wo"], dtype=f),
        "wbet": np.ascontiguousarray(inputs["L0_attn_Wbet"], dtype=f),
        "wf": np.ascontiguousarray(inputs["L0_attn_Wf"], dtype=f),
        "nwc": np.ascontiguousarray(
            np.asarray(inputs["L0_attn_norm_w"], dtype=f).reshape(8, 128).T),
        "gwt": np.ascontiguousarray(
            np.asarray(inputs["L0_ffn_gate_w"], dtype=f).T),
        "wg": np.ascontiguousarray(
            np.asarray(inputs["L0_ffn_w_gate"], dtype=f).transpose(0, 2, 1)),
        "wu": np.ascontiguousarray(
            np.asarray(inputs["L0_ffn_w_up"], dtype=f).transpose(0, 2, 1)),
        "wd": np.ascontiguousarray(
            np.asarray(inputs["L0_ffn_w_down"], dtype=f).transpose(0, 2, 1)),
        "sg": np.ascontiguousarray(
            np.asarray(inputs["L0_ffn_shared_gate_w"], dtype=f).T),
        "su": np.ascontiguousarray(
            np.asarray(inputs["L0_ffn_shared_up_w"], dtype=f).T),
        "sd": np.ascontiguousarray(
            np.asarray(inputs["L0_ffn_shared_down_w"], dtype=f).T),
    }
    eb = np.asarray(inputs["L0_ffn_expert_bias"], dtype=f)
    oneb = np.zeros((128, 2), dtype=f)
    oneb[:, 0] = 1.0
    oneb[:16, 1] = eb
    com["oneb"] = oneb
    com["ebrow"] = np.ascontiguousarray(
        np.broadcast_to(eb.reshape(1, 16), (128, 16)))

    nc = build_nc()
    in_maps = []
    for b in range(8):
        m = {"xt": np.ascontiguousarray(x[b].T)}
        m.update(com)
        in_maps.append(m)
    res = run_bass_kernel_spmd(nc, in_maps, core_ids=list(range(8)))
    if DEBUG_TAP is not None:
        return res.results
    outv = np.stack([np.asarray(res.results[b]["out"]).T for b in range(8)])
    return np.ascontiguousarray(outv.astype(np.float32))
